# version 01 (final)
# baseline (speedup 1.0000x reference)
import os
import contextlib
import numpy as np
import ml_dtypes
import concourse.bass as bass
import concourse.mybir as mybir
from concourse.bass_utils import run_bass_kernel_spmd

F32 = mybir.dt.float32
BF16 = mybir.dt.bfloat16
AF = mybir.ActivationFunctionType
ALU = mybir.AluOpType
AX = mybir.AxisListType

N = 2048
D = 1024
NT = 16
CT = 8
DH = 64
DIP = 2816
HCT = 22
NEG = -3.0e38

N_SUB = int(os.environ.get("MK_NSUB", "12"))
MK_PHASE = int(os.environ.get("MK_PHASE", "99"))
MK_PG = int(os.environ.get("MK_PG", "8"))
MK_PS = int(os.environ.get("MK_PS", "9"))
MK_PROJ = int(os.environ.get("MK_PROJ", "1"))

bfnp = ml_dtypes.bfloat16


def _devw(W):
    K, M = W.shape
    Kp = (K + 127) // 128 * 128
    if Kp != K:
        Wp = np.zeros((Kp, M), W.dtype)
        Wp[:K] = W
        W = Wp
    return np.ascontiguousarray(W.reshape(Kp // 128, 128, M).transpose(1, 0, 2))


def _hilo64(x):
    x = np.asarray(x, np.float64)
    hi = x.astype(np.float32)
    lo = (x - hi.astype(np.float64)).astype(np.float32)
    return hi, lo


def _hilo_bf(x):
    x = np.asarray(x, np.float32)
    hi = x.astype(bfnp).astype(np.float32)
    lo = (x - hi).astype(np.float32)
    return hi, lo


def host_arrays(inp):
    P = {}
    P["ident"] = np.eye(128, dtype=np.float32)
    P["onesrow"] = np.ones((1, 128), np.float32)
    P["onescol"] = np.ones((128, 1), np.float32)
    oA = np.zeros((128, 33), np.float32); oA[:, 0] = 1.0
    oB = np.zeros((128, 33), np.float32); oB[:, 32] = 1.0
    P["onesA"], P["onesB"] = oA, oB
    bd = np.zeros((128, 33), np.float32)
    bd[0:64, 0] = 1.0; bd[64:128, 32] = 1.0
    P["bd33"] = bd
    P["selh0"] = np.concatenate([np.ones((1, 64), np.float32), np.zeros((1, 64), np.float32)], 1)
    P["selh1"] = np.concatenate([np.zeros((1, 64), np.float32), np.ones((1, 64), np.float32)], 1)
    s4 = np.zeros((4, 256), np.float32)
    for hp in range(2):
        s4[hp, hp * 128:(hp + 1) * 128] = 1.0
        s4[2 + hp, hp * 128:(hp + 1) * 128] = 1.0
    P["sel4"] = s4
    r = np.arange(128)[:, None]; c = np.arange(128)[None, :]
    M0 = np.where(c < r, NEG, 0.0).astype(np.float32)
    M2 = np.where(c > r, NEG, 0.0).astype(np.float32)
    P["m0T"] = np.ascontiguousarray(M0.T)
    P["m2T"] = np.ascontiguousarray(M2.T)
    P["m0"] = np.ascontiguousarray(M0)
    P["m2"] = np.ascontiguousarray(M2)
    Prot = np.zeros((128, 128), np.float32)
    for base in (0, 64):
        for dd in range(32):
            Prot[base + dd, base + dd + 32] = -1.0
            Prot[base + dd + 32, base + dd] = 1.0
    P["protT"] = np.ascontiguousarray(Prot.T)
    half = np.arange(0, DH, 2, dtype=np.float64)
    invf = np.tile(np.concatenate([1.0 / 10000.0 ** (half / DH)] * 2), 2)
    base = np.tile(np.concatenate([(half + 0.4 * DH) / (1.4 * DH)] * 2), 2)
    ih, il = _hilo64(invf)
    P["invfhi"], P["invflo"] = ih.reshape(128, 1), il.reshape(128, 1)
    lh, ll = _hilo64(np.log(base))
    P["lnbhi"], P["lnblo"] = lh.reshape(128, 1), ll.reshape(128, 1)
    t = np.arange(N, dtype=np.float64)
    p = (t - N // 2) / 128.0
    P["pbcast"] = np.broadcast_to(p.astype(np.float32), (128, N)).copy()

    for nm in ("Wq", "Wk", "Wv", "Wproj"):
        P["p" + nm] = _devw(np.asarray(inp["L0_attn_" + nm]))
    for nm in ("bq", "bk", "bproj"):
        P["p" + nm] = np.ascontiguousarray(np.asarray(inp["L0_attn_" + nm]).reshape(8, 128).T)
    bvh, bvl = _hilo_bf(np.asarray(inp["L0_attn_bv"]))
    P["pbvrh"], P["pbvrl"] = bvh.reshape(1, D), bvl.reshape(1, D)
    P["wqf"] = np.vstack([np.asarray(inp["L0_attn_wq_feat"])] * 2)
    P["wkf"] = np.vstack([np.asarray(inp["L0_attn_wk_feat"])] * 2)

    Wqkv = np.asarray(inp["L0_ffn_Wqkv"]); Wo = np.asarray(inp["L0_ffn_Wo"])
    W1 = np.asarray(inp["L0_ffn_W1"]); W2 = np.asarray(inp["L0_ffn_W2"])
    b1 = np.asarray(inp["L0_ffn_b1"]); b2 = np.asarray(inp["L0_ffn_b2"])
    lng = np.asarray(inp["L0_ffn_ln_g"]); lnb = np.asarray(inp["L0_ffn_ln_b"])
    qs = np.asarray(inp["L0_ffn_q_scale"]); ks = np.asarray(inp["L0_ffn_k_scale"])
    for l in range(6):
        P[f"qkv{l}"] = _devw(Wqkv[l])
        P[f"wo{l}"] = _devw(Wo[l])
        W1a = np.zeros((D, DIP), np.float32); W1a[:, :2730] = W1[l][:, :2730]
        W1g = np.zeros((D, DIP), np.float32); W1g[:, :2730] = W1[l][:, 2730:]
        P[f"w1a{l}"] = _devw(W1a)
        P[f"w1g{l}"] = _devw(W1g)
        W2p = np.zeros((DIP, D), np.float32); W2p[:2730] = W2[l]
        P[f"w2{l}"] = _devw(W2p)
        b1a = np.zeros(DIP, np.float32); b1a[:2730] = b1[l][:2730]
        b1g = np.zeros(DIP, np.float32); b1g[:2730] = b1[l][2730:]
        P[f"b1a{l}"] = np.ascontiguousarray(b1a.reshape(HCT, 128).T)
        P[f"b1g{l}"] = np.ascontiguousarray(b1g.reshape(HCT, 128).T)
        P[f"b2_{l}"] = np.ascontiguousarray(b2[l].reshape(8, 128).T)
        P[f"lng{l}"] = np.ascontiguousarray(lng[l].reshape(8, 128).T)
        P[f"lnb{l}"] = np.ascontiguousarray(lnb[l].reshape(8, 128).T)
        P[f"qsk{l}"] = np.stack([np.concatenate([qs[l], qs[l]]) * 8.0,
                                 np.concatenate([ks[l], ks[l]])], axis=1).astype(np.float32)
    return P


class PSem:
    def __init__(self, h):
        self.h = h
        self.n = 0

    def inc(self, k=1):
        self.n += k
        return self.n


AR_COLS = 14336
ARMAP = {
    "vtokg": (0, 2048),
    "qp0": (2048, 2048),
    "qp1": (4096, 2048),
    "kp": (6144, 4096),
    "outtok": (10240, 512),
    "wql": (10752, 1024),
    "wkl": (11776, 1024),
    "wvl": (12800, 1024),
    "aib": (0, 4096),
    "wA0": (4096, 4096),
    "wA1": (8192, 4096),
    "rot": (2048, 2048),
    "xn": (0, 4096),
    "wa0": (4096, 1024),
    "wa1": (5120, 1024),
    "wg0": (6144, 1024),
    "wg1": (7168, 1024),
    "w20": (8192, 1024),
    "w21": (9216, 1024),
    "hgs0": (10240, 512),
    "hgs1": (10752, 512),
    "hgr0": (11264, 512),
    "hgr1": (11776, 512),
}


class B:
    def __init__(self):
        self.nc = bass.Bass("TRN2", target_bir_lowering=False)
        self.ctx = contextlib.ExitStack()
        self.E = {e: [] for e in ("t", "v", "s", "p", "y")}
        self.waited = {e: {} for e in self.E}
        self.dr = {}
        self.t = {}
        self.wlast = {}
        self._stage_io = 0
        self._mk_dram()
        self._mk_core1()
        s = lambda nm: PSem(self.ctx.enter_context(self.nc.semaphore(nm)))
        self.sT, self.sV, self.sS, self.sP = s("sT"), s("sV"), s("sS"), s("sP")
        self.sWD, self.sIO, self.sBar = s("sWD"), s("sIO"), s("sBar")
        self.bsem = {nm: s("sB_" + nm) for nm in
                     ("wql", "wkl", "wvl", "wA0", "wA1", "wa0", "wa1",
                      "wg0", "wg1", "w20", "w21", "hgr0", "hgr1", "qscol", "lparam")}

    def em(self, e, fn):
        self.E[e].append(fn)

    def wait(self, e, sem, val):
        if val <= 0:
            return
        key = id(sem.h)
        if self.waited[e].get(key, -1) >= val:
            return
        self.waited[e][key] = val
        self.em(e, lambda eng, h=sem.h, v=val: eng.wait_ge(h, v))

    def sem_for(self, e):
        return {"t": self.sT, "v": self.sV, "s": self.sS, "p": self.sP}[e]

    def tick(self, e, instr_fn):
        sem = self.sem_for(e)
        tgt = sem.inc()
        self.em(e, lambda eng, f=instr_fn, h=sem.h: f(eng).then_inc(h, 1))
        return tgt

    def tickd(self, e):
        return self.tick(e, lambda eng: eng.drain())

    def vgap(self):
        self.em("v", lambda e: e.drain())

    def sgap(self):
        self.em("s", lambda e: e.drain())

    def bar(self):
        self.wait("y", self.sIO, self.sIO.n)
        self.wait("p", self.sWD, self.sWD.n)
        tgt = self.sBar.n + 5
        self.sBar.n = tgt
        for e in self.E:
            if e in ("t", "v", "s"):
                def _f(eng, t=tgt, h=self.sBar.h):
                    eng.drain().then_inc(h, 1)
                    eng.wait_ge(h, t)
            else:
                def _f(eng, t=tgt, h=self.sBar.h):
                    eng.nop().then_inc(h, 1)
                    eng.wait_ge(h, t)
            self.em(e, _f)
        for e in self.E:
            for s in (self.sT, self.sV, self.sS, self.sP, self.sWD, self.sIO):
                self.waited[e][id(s.h)] = max(self.waited[e].get(id(s.h), -1), s.n)

    def dma(self, qe, out, in_, slow=False, sem=None):
        if sem is None:
            sem = self.sWD if qe == "p" else self.sIO
        tgt = sem.inc(16)
        kw = {"allow_slow_non_contiguous": True} if slow else {}
        self.em(qe, lambda eng, o=out, i=in_, h=sem.h, kw=kw:
                eng.dma_start(out=o, in_=i, **kw).then_inc(h, 16))
        return sem, tgt

    def ar(self, name, sub=None):
        off, ln = ARMAP[name]
        a = self.t["AR"][:, off:off + ln]
        if sub is not None:
            a = a[:, sub[0]:sub[1]]
        return a

    def load_w(self, bufname, out, in_):
        prev = self.wlast.get(bufname, 0)
        self.wait("p", self.sT, prev)
        sem = self.bsem[bufname]
        _, tgt = self.dma("p", out, in_, sem=sem)
        return sem, tgt

    def mark_w(self, bufname, tick):
        self.wlast[bufname] = max(self.wlast.get(bufname, 0), tick)

    def _mk_dram(self):
        nc = self.nc
        g = lambda name, shape, dt=F32, kind="ExternalInput": nc.dram_tensor(name, shape, dt, kind=kind).ap()
        dr = self.dr
        dr["xT"] = g("xT", [D, N])
        dr["yT"] = g("yT", [D, N], kind="ExternalOutput")
        for nm, shape in [("ident", [128, 128]), ("onesrow", [1, 128]), ("onescol", [128, 1]),
                          ("onesA", [128, 33]), ("onesB", [128, 33]), ("bd33", [128, 33]),
                          ("selh0", [1, 128]), ("selh1", [1, 128]), ("sel4", [4, 256]),
                          ("m0T", [128, 128]), ("m2T", [128, 128]),
                          ("m0", [128, 128]), ("m2", [128, 128]), ("protT", [128, 128]),
                          ("invfhi", [128, 1]), ("invflo", [128, 1]),
                          ("lnbhi", [128, 1]), ("lnblo", [128, 1]), ("pbcast", [128, N])]:
            dr[nm] = g(nm, shape)
        for nm in ("pWq", "pWk", "pWv", "pWproj"):
            dr[nm] = g(nm, [128, 8, 1024])
        for nm in ("pbq", "pbk", "pbproj"):
            dr[nm] = g(nm, [128, 8])
        dr["pbvrh"] = g("pbvrh", [1, D]); dr["pbvrl"] = g("pbvrl", [1, D])
        dr["wqf"] = g("wqf", [128, 256]); dr["wkf"] = g("wkf", [128, 256])
        for l in range(6):
            dr[f"qkv{l}"] = g(f"qkv{l}", [128, 8, 3072])
            dr[f"wo{l}"] = g(f"wo{l}", [128, 8, 1024])
            dr[f"w1a{l}"] = g(f"w1a{l}", [128, 8, DIP])
            dr[f"w1g{l}"] = g(f"w1g{l}", [128, 8, DIP])
            dr[f"w2{l}"] = g(f"w2{l}", [128, HCT, 1024])
            for nm in (f"b1a{l}", f"b1g{l}"):
                dr[nm] = g(nm, [128, HCT])
            for nm in (f"b2_{l}", f"lng{l}", f"lnb{l}"):
                dr[nm] = g(nm, [128, 8])
            dr[f"qsk{l}"] = g(f"qsk{l}", [128, 2])
        dr["dbg"] = g("dbg", [128, 600], kind="ExternalOutput")
        dr["x1sav"] = g("x1sav", [D, N], kind="Internal")
        dr["aob"] = nc.dram_tensor("aob", [D, N], BF16, kind="Internal").ap()
        dr["hgb"] = nc.dram_tensor("hgb", [DIP, 512], BF16, kind="Internal").ap()

    def _mk_core1(self):
        sb = lambda name, shape, dt: self.ctx.enter_context(self.nc.sbuf_tensor(name, shape, dt))
        ps = lambda name, shape, dt: self.ctx.enter_context(self.nc.psum_tensor(name, shape, dt))
        t = self.t
        for cc in range(CT):
            t[f"XF{cc}"] = sb(f"XF{cc}", [128, N + 4], F32)
            t[f"XB{cc}"] = sb(f"XB{cc}", [128, N + 4], BF16)
        for nm, shape in [("identb", [128, 128]), ("ones1", [1, 128]), ("onescolb", [128, 1]),
                          ("onesAb", [128, 33]), ("onesBb", [128, 33]), ("bd33b", [128, 33]),
                          ("selh0b", [1, 128]), ("selh1b", [1, 128]), ("sel4b", [4, 256]),
                          ("m0Tb", [128, 128]), ("m2Tb", [128, 128]),
                          ("m0b", [128, 128]), ("m2b", [128, 128]),
                          ("protTb", [128, 128])]:
            t[nm] = sb(nm, shape, BF16)
        t["consts"] = sb("consts", [128, 8], F32)
        t["dbg1"] = sb("dbg1", [128, 288], F32)
        for nm in ("cs2q", "ssnq", "cs2k", "ssnk"):
            t[nm] = sb(nm, [128, N], BF16)
        t["piv"] = sb("piv", [128, 8], F32)
        self.pb = [ps(f"pb{i}", [128, 512], F32) for i in range(7)]
        self.pbb = ps("pbb", [128, 1024], BF16)

    def mk_gen_scratch(self, stack):
        sb = lambda name, shape, dt: stack.enter_context(self.nc.sbuf_tensor(name, shape, dt))
        t = self.t
        t["pbc"] = sb("pbc", [128, N], F32)
        t["tgA"] = sb("tgA", [128, N], F32)
        t["tgB"] = sb("tgB", [128, N], F32)
        t["tgC"] = sb("tgC", [128, N], F32)
        t["tgD"] = sb("tgD", [128, N], F32)
        t["tgS"] = sb("tgS", [128, 1024], F32)
        t["tgS2"] = sb("tgS2", [128, 1024], F32)

    def mk_core2(self):
        sb = lambda name, shape, dt: self.ctx.enter_context(self.nc.sbuf_tensor(name, shape, dt))
        t = self.t
        t["qTg"] = sb("qTg", [128, N], BF16)
        t["kTg"] = sb("kTg", [128, N], BF16)
        t["wqfb"] = sb("wqfb", [128, 256], BF16)
        t["wkfb"] = sb("wkfb", [128, 256], BF16)
        t["ksumb"] = sb("ksumb", [128, 8], BF16)
        t["ctxb"] = sb("ctxb", [128, 256], BF16)
        t["dinv"] = sb("dinv", [128, 32], F32)
        t["stage"] = sb("stage", [128, 1024], BF16)
        t["pbias"] = sb("pbias", [128, 24], F32)
        t["bvrh"] = sb("bvrh", [1, D], BF16)
        t["bvrl"] = sb("bvrl", [1, D], BF16)
        t["gelu"] = sb("gelu", [128, 512], BF16)
        t["b1a"] = sb("b1a", [128, HCT], F32)
        t["b1g"] = sb("b1g", [128, HCT], F32)
        t["lparam"] = sb("lparam", [128, 24], F32)
        t["qscol"] = sb("qscol", [128, 2], F32)
        t["SS"] = sb("SS", [128, 16], F32)
        t["SSb"] = sb("SSb", [128, 4], BF16)
        t["shiftT"] = sb("shiftT", [4, N], BF16)
        t["pT"] = sb("pT", [128, 384], BF16)
        t["junkb"] = sb("junkb", [128, 384], BF16)
        t["rowS"] = sb("rowS", [1, N], F32)
        t["rowQ"] = sb("rowQ", [1, N], F32)
        t["invh"] = sb("invh", [1, N], BF16)
        t["invl"] = sb("invl", [1, N], BF16)
        t["rsh"] = sb("rsh", [1, N], BF16)
        t["rsl"] = sb("rsl", [1, N], BF16)
        t["scr"] = sb("scr", [128, 512], F32)
        t["sqb"] = sb("sqb", [128, 2048], BF16)
        t["junk"] = sb("junk", [128, 8], F32)
        t["AR"] = sb("AR", [128, AR_COLS], BF16)
        t["dbgt"] = sb("dbgt", [128, 600], F32)
        t["muh"] = t["invh"]
        t["mul"] = t["invl"]

    def ph_load_pre(self):
        t, dr = self.t, self.dr
        for cc in range(CT):
            self.dma("y", t[f"XF{cc}"][:, 1:N + 1], dr["xT"][cc * 128:(cc + 1) * 128, :])
            self.dma("p", t[f"XB{cc}"][:, 1:N + 1], dr["xT"][cc * 128:(cc + 1) * 128, :])
            for nm in (f"XF{cc}", f"XB{cc}"):
                self.em("v", lambda eng, a=t[nm]: eng.memset(a[:, 0:1], 0.0))
                self.em("v", lambda eng, a=t[nm]: eng.memset(a[:, N + 1:N + 4], 0.0))
        for nm, tgt in [("ident", "identb"), ("onesrow", "ones1"), ("onescol", "onescolb"),
                        ("onesA", "onesAb"), ("onesB", "onesBb"), ("bd33", "bd33b"),
                        ("selh0", "selh0b"), ("selh1", "selh1b"), ("sel4", "sel4b"),
                        ("m0T", "m0Tb"), ("m2T", "m2Tb"), ("m0", "m0b"), ("m2", "m2b"),
                        ("protT", "protTb")]:
            self.dma("p", t[tgt][:], dr[nm])
        for i, nm in enumerate(("invfhi", "invflo", "lnbhi", "lnblo")):
            self.dma("y", t["consts"][:, i:i + 1], dr[nm])
        self.dma("y", t["pbc"][:], dr["pbcast"])
        self.bar()

    def ph_load_post(self):
        t, dr = self.t, self.dr
        self.dma("p", t["wqfb"][:], dr["wqf"])
        self.dma("p", t["wkfb"][:], dr["wkf"])
        for i, nm in enumerate(("pbq", "pbk", "pbproj")):
            self.dma("y", t["pbias"][:, i * 8:(i + 1) * 8], dr[nm])
        self.dma("p", t["bvrh"][:], dr["pbvrh"])
        self.dma("p", t["bvrl"][:], dr["pbvrl"])
        self.em("v", lambda e: e.memset(t["dbgt"][:], 0.0))
        self.vgap()
        if os.environ.get("MK_DBG") == "7":
            self.em("v", lambda e: e.tensor_copy(t["dbgt"][:, 0:128], t["cs2q"][:, 0:128]))
            self.em("v", lambda e: e.tensor_copy(t["dbgt"][:, 128:256], t["ssnq"][:, 0:128]))
            self.em("v", lambda e: e.tensor_copy(t["dbgt"][:, 256:544], t["dbg1"][:]))
        self.bar()

    def ph_tables(self):
        t = self.t
        co, si, sc, iv = t["tgA"], t["tgB"], t["tgC"], t["tgD"]
        s1, s2, piv, C = t["tgS"], t["tgS2"], t["piv"], t["consts"]
        self.em("v", lambda e: e.tensor_scalar_mul(sc[:], t["pbc"][:], C[:, 2:3]))
        self.vgap()
        self.em("v", lambda e: e.scalar_tensor_tensor(sc[:], t["pbc"][:], C[:, 3:4], sc[:],
                                                      op0=ALU.mult, op1=ALU.add))
        self.em("v", lambda e: e.memset(co[:, 0:1], 1.0))
        self.em("v", lambda e: e.memset(si[:, 0:1], 0.0))
        self.em("v", lambda e: e.memset(piv[:, 3:4], 1.5707963705062866))
        self.em("v", lambda e: e.tensor_add(piv[:, 2:3], C[:, 0:1], C[:, 1:2]))
        vt = self.tickd("v")
        self.wait("s", self.sV, vt)
        self.em("s", lambda e: e.activation(si[:, 1:2], piv[:, 2:3], AF.Sin))
        self.em("s", lambda e: e.activation(co[:, 1:2], piv[:, 2:3], AF.Sin, bias=piv[:, 3:4]))
        self.em("s", lambda e: e.activation(iv[:], sc[:], AF.Exp, scale=-1.0))
        self.sgap()
        self.em("s", lambda e: e.activation(sc[:], sc[:], AF.Exp))
        st = self.tickd("s")
        self.wait("v", self.sS, st)
        M = 1
        while M < N:
            hf = min(M, N - M)
            self.em("v", lambda e, M=M: e.tensor_mul(piv[:, 4:5], co[:, M - 1:M], co[:, 1:2]))
            self.em("v", lambda e, M=M: e.tensor_mul(piv[:, 5:6], si[:, M - 1:M], si[:, 1:2]))
            self.em("v", lambda e, M=M: e.tensor_mul(piv[:, 6:7], si[:, M - 1:M], co[:, 1:2]))
            self.em("v", lambda e, M=M: e.tensor_mul(piv[:, 7:8], co[:, M - 1:M], si[:, 1:2]))
            self.vgap()
            self.em("v", lambda e: e.tensor_sub(piv[:, 0:1], piv[:, 4:5], piv[:, 5:6]))
            self.em("v", lambda e: e.tensor_add(piv[:, 1:2], piv[:, 6:7], piv[:, 7:8]))
            self.vgap()
            self.em("v", lambda e, hf=hf: e.tensor_scalar_mul(s1[:, 0:hf], co[:, 0:hf], piv[:, 0:1]))
            self.em("v", lambda e, hf=hf: e.tensor_scalar_mul(s2[:, 0:hf], si[:, 0:hf], piv[:, 1:2]))
            self.em("v", lambda e, M=M, hf=hf: e.tensor_scalar_mul(si[:, M:M + hf], si[:, 0:hf], piv[:, 0:1]))
            self.em("v", lambda e, M=M, hf=hf: e.tensor_scalar_mul(co[:, M:M + hf], co[:, 0:hf], piv[:, 1:2]))
            self.vgap()
            self.em("v", lambda e, M=M, hf=hf: e.tensor_add(si[:, M:M + hf], si[:, M:M + hf], co[:, M:M + hf]))
            self.vgap()
            self.em("v", lambda e, M=M, hf=hf: e.tensor_sub(co[:, M:M + hf], s1[:, 0:hf], s2[:, 0:hf]))
            self.vgap()
            M += hf
        if os.environ.get("MK_DBG") == "7":
            self.em("v", lambda e: e.memset(t["dbg1"][:], 0.0))
            self.vgap()
            def dmp(e):
                e.tensor_copy(t["dbg1"][:, 0:4], t["consts"][:, 0:4])
                e.tensor_copy(t["dbg1"][:, 8:72], t["pbc"][:, 0:64])
                e.tensor_copy(t["dbg1"][:, 72:136], co[:, 0:64])
                e.tensor_copy(t["dbg1"][:, 136:200], si[:, 0:64])
                e.tensor_copy(t["dbg1"][:, 200:264], sc[:, 0:64])
                e.tensor_copy(t["dbg1"][:, 264:272], piv[:])
            self.em("v", dmp)
        def prods(e):
            e.tensor_mul(t["cs2q"][:], co[:], sc[:])
            e.tensor_mul(t["cs2k"][:], co[:], iv[:])
            e.tensor_mul(t["ssnq"][:], si[:], sc[:])
            e.tensor_mul(t["ssnk"][:], si[:], iv[:])
        self.em("v", prods)
        self.bar()

    def ln_rows(self, shifted):
        t, pb = self.t, self.pb
        off = (lambda cc: 0 if (shifted and cc >= 4) else 1)
        bank = pb[6]
        prev_copy = 0
        bt = 0
        for ng in range(4):
            c0 = ng * 512
            self.wait("v", self.sV, prev_copy)
            for batch in range(2):
                self.wait("v", self.sT, bt)
                for cc in range(batch * 4, batch * 4 + 4):
                    xa = t[f"XB{cc}"][:, c0 + off(cc): c0 + off(cc) + 512]
                    self.em("v", lambda e, a=xa, cc=cc: e.tensor_mul(
                        t["sqb"][:, (cc % 4) * 512:(cc % 4) * 512 + 512], a, a))
                vt = self.tickd("v")
                self.wait("t", self.sV, vt)
                for cc in range(batch * 4, batch * 4 + 4):
                    xa = t[f"XB{cc}"][:, c0 + off(cc): c0 + off(cc) + 512]
                    sq = t["sqb"][:, (cc % 4) * 512:(cc % 4) * 512 + 512]
                    self.em("t", lambda e, a=xa, st=(cc == 0): e.matmul(
                        bank.ap()[0:33, 0:512], t["onesAb"][:], a, start=st, stop=False))
                    self.em("t", lambda e, a=sq, last=(cc == CT - 1): e.matmul(
                        bank.ap()[0:33, 0:512], t["onesBb"][:], a, start=False, stop=last))
                bt = self.tickd("t")
            mt = bt
            self.wait("v", self.sT, mt)
            self.em("v", lambda e, c0=c0: e.tensor_copy(t["rowS"][0:1, c0:c0 + 512], bank.ap()[0:1, 0:512]))
            self.em("v", lambda e, c0=c0: e.tensor_copy(t["rowQ"][0:1, c0:c0 + 512], bank.ap()[32:33, 0:512]))
            prev_copy = self.tickd("v")
            self.wait("t", self.sV, prev_copy)
        self.em("v", lambda e: e.tensor_scalar_mul(t["rowS"][:], t["rowS"][:], 1.0 / D))
        self.em("v", lambda e: e.tensor_scalar_mul(t["rowQ"][:], t["rowQ"][:], 1.0 / D))
        self.vgap()
        for ng in range(4):
            c0 = ng * 512
            self.em("v", lambda e, c0=c0: e.tensor_mul(t["scr"][0:1, 0:512],
                                                       t["rowS"][0:1, c0:c0 + 512],
                                                       t["rowS"][0:1, c0:c0 + 512]))
            self.vgap()
            self.em("v", lambda e, c0=c0: e.tensor_sub(t["rowQ"][0:1, c0:c0 + 512],
                                                       t["rowQ"][0:1, c0:c0 + 512],
                                                       t["scr"][0:1, 0:512]))
            self.vgap()
        self.em("v", lambda e: e.tensor_copy(t["muh"][:], t["rowS"][:]))
        self.vgap()
        self.em("v", lambda e: e.tensor_sub(t["mul"][:], t["rowS"][:], t["muh"][:]))
        self.em("v", lambda e: e.memset(t["junk"][0:1, 0:1], 1e-5))
        vt = self.tickd("v")
        self.wait("s", self.sV, vt)
        self.em("s", lambda e: e.activation(t["rowQ"][:], t["rowQ"][:], AF.Ln, bias=t["junk"][0:1, 0:1]))
        self.sgap()
        self.em("s", lambda e: e.activation(t["rowQ"][:], t["rowQ"][:], AF.Exp, scale=-0.5))
        st = self.tickd("s")
        self.wait("v", self.sS, st)
        self.em("v", lambda e: e.tensor_copy(t["rsh"][:], t["rowQ"][:]))
        self.vgap()
        self.em("v", lambda e: e.tensor_sub(t["rsl"][:], t["rowQ"][:], t["rsh"][:]))
        if os.environ.get("MK_DBG") == "3" and not getattr(self, "_lndumped", False):
            self._lndumped = True
            self.em("v", lambda e: e.tensor_copy(t["dbgt"][0:1, 0:480], t["rowS"][0:1, 0:480]))
            self.em("v", lambda e: e.tensor_copy(t["dbgt"][0:1, 480:600], t["rowQ"][0:1, 0:120]))
        return self.tickd("v")

    def ln_bcast(self, ng):
        t, pb = self.t, self.pb
        c0 = ng * 512
        self.em("t", lambda e: e.matmul(pb[5].ap()[:], t["ones1"][:], t["muh"][0:1, c0:c0 + 512], start=True, stop=False))
        self.em("t", lambda e: e.matmul(pb[5].ap()[:], t["ones1"][:], t["mul"][0:1, c0:c0 + 512], start=False, stop=True))
        self.em("t", lambda e: e.matmul(pb[6].ap()[:], t["ones1"][:], t["rsh"][0:1, c0:c0 + 512], start=True, stop=False))
        return self.tick("t", lambda e: e.matmul(pb[6].ap()[:], t["ones1"][:], t["rsl"][0:1, c0:c0 + 512], start=False, stop=True))

    def ln_apply_block(self, save_x1):
        t, pb = self.t, self.pb
        rt = self.ln_rows(shifted=False)
        self.wait("t", self.sV, rt)
        for ng in range(4):
            c0 = ng * 512
            mt = self.ln_bcast(ng)
            self.wait("v", self.sT, mt)
            for cc in range(CT):
                xfa = t[f"XF{cc}"][:, c0 + 1: c0 + 513]
                self.em("v", lambda e, a=xfa: e.tensor_sub(t["scr"][:], a, pb[5].ap()[:]))
                self.vgap()
                self.em("v", lambda e, a=xfa: e.tensor_mul(a, t["scr"][:], pb[6].ap()[:]))
                self.vgap()
                self.em("v", lambda e, cc=cc, c0=c0: e.tensor_copy(
                    t[f"XB{cc}"][:, c0 + 1: c0 + 513], t[f"XF{cc}"][:, c0 + 1: c0 + 513]))
            vt = self.tickd("v")
            self.wait("t", self.sV, vt)
            if save_x1:
                self.wait("y", self.sV, vt)
                for cc in range(CT):
                    self.dma("y", self.dr["x1sav"][cc * 128:(cc + 1) * 128, c0:c0 + 512],
                             t[f"XF{cc}"][:, c0 + 1:c0 + 513])

    def ln_apply_xn(self, nb):
        t, pb = self.t, self.pb
        c0 = nb * 512
        off = (lambda cc: 0 if cc >= 4 else 1)
        mt = self.ln_bcast(nb)
        self.wait("v", self.sT, mt)
        for cc in range(CT):
            xfa = t[f"XF{cc}"][:, c0 + off(cc): c0 + off(cc) + 512]
            g_ap = t["lparam"][:, 8 + cc:9 + cc]
            b_ap = t["lparam"][:, 16 + cc:17 + cc]
            self.em("v", lambda e, a=xfa: e.tensor_sub(t["scr"][:], a, pb[5].ap()[:]))
            self.vgap()
            self.em("v", lambda e: e.tensor_mul(t["scr"][:], t["scr"][:], pb[6].ap()[:]))
            self.vgap()
            self.em("v", lambda e, g=g_ap, bb=b_ap, cc=cc: e.tensor_scalar(
                self.ar("xn", (cc * 512, cc * 512 + 512)), t["scr"][:], g, bb,
                op0=ALU.mult, op1=ALU.add))
        vt = self.tickd("v")
        self.wait("t", self.sV, vt)
        return vt

    def qk_produce(self, wbn, dst, wtv, bcol, xoff):
        t, pb = self.t, self.pb
        wbuf = self.ar(wbn)
        self.wait("t", wtv[0], wtv[1])
        for ng in range(4):
            bank = pb[2 + (ng % 2)]
            for cc in range(CT):
                xa = t[f"XB{cc}"][:, xoff(cc) + ng * 512: xoff(cc) + ng * 512 + 512]
                self.em("t", lambda e, w=wbuf, cc=cc, a=xa, st=(cc == 0), b=bank:
                        e.matmul(b.ap()[:], w[:, cc * 128:cc * 128 + 128], a,
                                 start=st, stop=(cc == CT - 1)))
            mt = self.tickd("t")
            self.mark_w(wbn, mt)
            self.wait("v", self.sT, mt)
            dd = dst[:, ng * 512:(ng + 1) * 512]
            if bcol is not None:
                et = self.tick("v", lambda e, d=dd, b=bank, bc=bcol: e.tensor_scalar(
                    d, b.ap()[:], bc, None, op0=ALU.add))
            else:
                et = self.tick("v", lambda e, d=dd, b=bank: e.tensor_copy(d, b.ap()[:]))
            self.wait("t", self.sV, et)

    def v_produce(self, wtv, xoff, g, bias):
        t, pb = self.t, self.pb
        self.wait("t", wtv[0], wtv[1])
        wv = self.ar("wvl")
        for nt4 in range(4):
            bank = pb[2 + (nt4 % 2)]
            for k in range(4):
                nt = nt4 * 4 + k
                for cc in range(CT):
                    xa = t[f"XB{cc}"][:, xoff(cc) + nt * 128: xoff(cc) + nt * 128 + 128]
                    self.em("t", lambda e, a=xa, w=wv, cc=cc, k=k, st=(cc == 0), b=bank:
                            e.matmul(b.ap()[:, k * 128:(k + 1) * 128], a,
                                     w[:, cc * 128:cc * 128 + 128],
                                     start=st, stop=(False if bias else cc == CT - 1)))
                if bias:
                    self.em("t", lambda e, k=k, g=g, b=bank: e.matmul(
                        b.ap()[:, k * 128:(k + 1) * 128], t["ones1"][:],
                        t["bvrh"][0:1, g * 128:g * 128 + 128], start=False, stop=False))
                    self.em("t", lambda e, k=k, g=g, b=bank: e.matmul(
                        b.ap()[:, k * 128:(k + 1) * 128], t["ones1"][:],
                        t["bvrl"][0:1, g * 128:g * 128 + 128], start=False, stop=True))
            mt = self.tickd("t")
            self.mark_w("wvl", mt)
            self.wait("v", self.sT, mt)
            dst = self.ar("vtokg", (nt4 * 512, nt4 * 512 + 512))
            et = self.tick("v", lambda e, d=dst, b=bank: e.tensor_copy(d, b.ap()[:]))
            self.wait("t", self.sV, et)

    def ph_performer(self):
        t, pb, dr = self.t, self.pb, self.dr
        ident = t["identb"]
        for g in range(MK_PG):
            qt_ = self.load_w("wql", self.ar("wql").rearrange("p (c w) -> p c w", c=8),
                              dr["pWq"][:, :, g * 128:(g + 1) * 128])
            kt_ = self.load_w("wkl", self.ar("wkl").rearrange("p (c w) -> p c w", c=8),
                              dr["pWk"][:, :, g * 128:(g + 1) * 128])
            vt_ = self.load_w("wvl", self.ar("wvl").rearrange("p (c w) -> p c w", c=8),
                              dr["pWv"][:, :, g * 128:(g + 1) * 128])
            unsh = lambda cc: 1
            if MK_PS < 1:
                continue
            self.qk_produce("wql", t["qTg"], qt_, t["pbias"][:, g:g + 1], unsh)
            self.qk_produce("wkl", t["kTg"], kt_, t["pbias"][:, 8 + g:9 + g], unsh)
            if MK_PS < 2:
                continue
            self.v_produce(vt_, unsh, g, bias=True)
            if MK_PS < 3:
                continue
            for hp in range(2):
                for fc in range(2):
                    for ng in range(4):
                        bank = pb[4 + (ng % 2)]
                        mt = self.tick("t", lambda e, hp=hp, fc=fc, ng=ng, b=bank: e.matmul(
                            b.ap()[:], t["wqfb"][hp * 64:hp * 64 + 64, fc * 128:(fc + 1) * 128],
                            t["qTg"][hp * 64:hp * 64 + 64, ng * 512:(ng + 1) * 512],
                            start=True, stop=True))
                        self.wait("s", self.sT, mt)
                        dst = self.ar(f"qp{fc}", (ng * 512, ng * 512 + 512))
                        et = self.tick("s", lambda e, d=dst, b=bank: e.activation(d, b.ap()[:], AF.Exp))
                        self.wait("t", self.sS, et)
                if MK_PS < 4:
                    continue
                for nt2 in range(8):
                    bank = pb[4 + (nt2 % 2)]
                    for k in range(2):
                        nt = nt2 * 2 + k
                        self.em("t", lambda e, nt=nt, hp=hp, k=k, b=bank, st=(k == 0): e.matmul(
                            b.ap()[:, k * 256:(k + 1) * 256],
                            t["kTg"][hp * 64:hp * 64 + 64, nt * 128:(nt + 1) * 128],
                            t["wkfb"][hp * 64:hp * 64 + 64, :], start=st, stop=(k == 1)))
                    mt = self.tickd("t")
                    self.wait("s", self.sT, mt)
                    dst = self.ar("kp", (nt2 * 512, nt2 * 512 + 512))
                    et = self.tick("s", lambda e, d=dst, b=bank: e.activation(d, b.ap()[:], AF.Exp))
                    self.wait("t", self.sS, et)
                if MK_PS < 5:
                    continue
                bank = pb[6]
                first = True
                for nt in range(NT):
                    for fc in range(2):
                        ks = self.ar("kp", (nt * 256 + fc * 128, nt * 256 + fc * 128 + 128))
                        vs = self.ar("vtokg", (nt * 128 + hp * 64, nt * 128 + hp * 64 + 64))
                        self.em("t", lambda e, ks=ks, vs=vs, fc=fc, st=first, b=bank:
                                e.matmul(b.ap()[:, fc * 64:fc * 64 + 64], ks, vs, start=st, stop=False))
                        first = False
                        self.em("t", lambda e, ks=ks, fc=fc, b=bank, last=(nt == NT - 1 and fc == 1):
                                e.matmul(b.ap()[:, 256 + fc:256 + fc + 1], ks, t["onescolb"][:],
                                         start=False, stop=last))
                mt = self.tickd("t")
                self.wait("v", self.sT, mt)
                self.em("v", lambda e, b=bank: e.tensor_copy(t["ctxb"][:, 0:128], b.ap()[:, 0:128]))
                et = self.tick("v", lambda e, b=bank: e.tensor_copy(t["ksumb"][:, 0:2], b.ap()[:, 256:258]))
                self.wait("t", self.sV, et)
                if MK_PS < 6:
                    continue
                for nt in range(NT):
                    ob = pb[nt // 8]
                    for fc in range(2):
                        qsl = self.ar(f"qp{fc}", (nt * 128, nt * 128 + 128))
                        st = (fc == 0)
                        self.em("t", lambda e, q=qsl, fc=fc, st=st, ob=ob, nt=nt:
                                e.matmul(ob.ap()[:, (nt % 8) * 64:(nt % 8) * 64 + 64], q,
                                         t["ctxb"][:, fc * 64:fc * 64 + 64],
                                         start=st, stop=(fc == 1)))
                        self.em("t", lambda e, q=qsl, fc=fc, st=st, nt=nt:
                                e.matmul(pb[4].ap()[:, nt:nt + 1], q,
                                         t["ksumb"][:, fc:fc + 1],
                                         start=st, stop=(fc == 1)))
                mt = self.tickd("t")
                self.wait("v", self.sT, mt)
                if g == 0 and hp == 0 and os.environ.get("MK_DBG") == "1":
                    self.em("v", lambda e: e.tensor_copy(t["dbgt"][:, 0:16], pb[4].ap()[:, 0:16]))
                    self.em("v", lambda e: e.tensor_copy(t["dbgt"][:, 16:144], t["ctxb"][:, 0:128]))
                    self.em("v", lambda e: e.tensor_copy(t["dbgt"][:, 144:146], t["ksumb"][:, 0:2]))
                    self.em("v", lambda e: e.tensor_copy(t["dbgt"][:, 146:274], self.ar("qp0", (0, 128))))
                    self.em("v", lambda e: e.tensor_copy(t["dbgt"][:, 274:402], self.ar("kp", (0, 128))))
                    self.em("v", lambda e: e.tensor_copy(t["dbgt"][:, 402:530], t["qTg"][:, 0:128]))
                    self.em("v", lambda e: e.tensor_copy(t["dbgt"][:, 530:658], t["kTg"][:, 0:128]))
                    self.em("v", lambda e: e.tensor_copy(t["dbgt"][:, 658:786], self.ar("vtokg", (0, 128))))
                self.em("v", lambda e: e.tensor_scalar_add(t["dinv"][:, 0:16], pb[4].ap()[:, 0:16], 1e-6))
                self.tickd("v")
                self.em("v", lambda e: e.reciprocal(t["dinv"][:, 16:32], t["dinv"][:, 0:16]))
                self.tickd("v")
                if g == 0 and hp == 0 and os.environ.get("MK_DBG") == "1":
                    self.em("v", lambda e: e.tensor_copy(t["dbgt"][:, 786:802], t["dinv"][:, 16:32]))
                if MK_PS < 7:
                    continue
                for half in range(2):
                    for k in range(8):
                        nt = half * 8 + k
                        ob = pb[nt // 8]
                        dst = self.ar("outtok", (k * 64, k * 64 + 64))
                        self.em("v", lambda e, d=dst, ob=ob, nt=nt: e.tensor_scalar(
                            d, ob.ap()[:, (nt % 8) * 64:(nt % 8) * 64 + 64],
                            t["dinv"][:, 16 + nt:17 + nt], None, op0=ALU.mult))
                    if g == 0 and hp == 0 and half == 0 and os.environ.get("MK_DBG") == "1":
                        self.em("v", lambda e: e.tensor_copy(t["dbgt"][:, 256:384], self.ar("outtok", (0, 128))))
                    vt2 = self.tickd("v")
                    self.wait("t", self.sV, vt2)
                    for k in range(8):
                        src = self.ar("outtok", (k * 64, k * 64 + 64))
                        self.em("t", lambda e, s_=src, k=k: e.transpose(
                            self.pbb.ap()[0:64, k * 128:(k + 1) * 128], s_, ident[:]))
                    mt = self.tickd("t")
                    self.wait("v", self.sT, mt)
                    self.wait("v", self.sIO, self._stage_io)
                    et = self.tick("v", lambda e: e.tensor_copy(t["stage"][0:64, :], self.pbb.ap()[0:64, :]))
                    self.wait("t", self.sV, et)
                    self.wait("y", self.sV, et)
                    _, siot = self.dma("y", dr["aob"][g * 128 + hp * 64: g * 128 + hp * 64 + 64,
                                                      half * 1024:(half + 1) * 1024],
                                       t["stage"][0:64, :])
                    self._stage_io = siot
        self.bar()
        if MK_PROJ:
            self.proj_residual(dr["pWproj"], t["pbias"], 16, refresh_xb=True)
        self.bar()

    def proj_residual(self, wdram, bias_tile, bias_off, refresh_xb):
        t, pb, dr = self.t, self.pb, self.dr
        wts = []
        for pg in range(2):
            wbn = "wA0" if pg == 0 else "wA1"
            wt = self.load_w(wbn, self.ar(wbn).rearrange("p (c w) -> p c w", c=8),
                             wdram[:, :, pg * 512:(pg + 1) * 512])
            wts.append((wbn, wt))
        last_mm = 0
        for ng in range(4):
            self.wait("y", self.sT, last_mm)
            ats = []
            for cc in range(CT):
                dstc = self.ar("aib", (cc * 512, cc * 512 + 512))
                _, at = self.dma("y", dstc, dr["aob"][cc * 128:(cc + 1) * 128, ng * 512:(ng + 1) * 512])
                ats.append(at)
            self.wait("t", self.sIO, ats[-1])
            for pg in range(2):
                wbn, wt = wts[pg]
                wb = self.ar(wbn)
                self.wait("t", wt[0], wt[1])
                for oc in range(4):
                    octile = pg * 4 + oc
                    bank = pb[(ng * 4 + oc) % 2]
                    for cc in range(CT):
                        aibc = self.ar("aib", (cc * 512, cc * 512 + 512))
                        self.em("t", lambda e, w=wb, cc=cc, oc=oc, a=aibc, st=(cc == 0), b=bank:
                                e.matmul(b.ap()[:], w[:, cc * 512 + oc * 128: cc * 512 + oc * 128 + 128],
                                         a, start=st, stop=(cc == CT - 1)))
                    mt = self.tickd("t")
                    last_mm = mt
                    self.wait("v", self.sT, mt)
                    if ng == 0 and pg == 0 and oc == 0 and os.environ.get("MK_DBG") == "2":
                        self.em("v", lambda e: e.tensor_copy(t["dbgt"][:, 0:512], self.ar("aib", (0, 512))))
                        self.em("v", lambda e, w=wb: e.tensor_copy(t["dbgt"][:, 512:640], w[:, 0:128]))
                        self.em("v", lambda e, b=bank: e.tensor_copy(t["dbgt"][:, 640:768], b.ap()[:, 0:128]))
                    xfa = t[f"XF{octile}"][:, ng * 512 + 1: ng * 512 + 513]
                    if bias_tile is not None:
                        bcol = bias_tile[:, bias_off + octile:bias_off + octile + 1]
                        et = self.tick("v", lambda e, o=xfa, b=bank, bc=bcol: e.scalar_tensor_tensor(
                            o, b.ap()[:], bc, o, op0=ALU.add, op1=ALU.add))
                    else:
                        et = self.tick("v", lambda e, o=xfa, b=bank: e.tensor_add(o, b.ap()[:], o))
                    if refresh_xb:
                        self.vgap()
                        self.em("v", lambda e, octile=octile, ng=ng: e.tensor_copy(
                            t[f"XB{octile}"][:, ng * 512 + 1: ng * 512 + 513],
                            t[f"XF{octile}"][:, ng * 512 + 1: ng * 512 + 513]))
                        et = self.tickd("v")
                    self.wait("t", self.sV, et)
                    if ng == 0 and pg == 0 and oc == 0 and os.environ.get("MK_DBG") == "2":
                        self.em("v", lambda e, octile=octile: e.tensor_copy(
                            t["dbgt"][:, 768:896], t[f"XF{octile}"][:, 1:129]))
        self.mark_w("wA0", last_mm)
        self.mark_w("wA1", last_mm)

    def ph_inner_attn(self, l):
        t, pb, dr = self.t, self.pb, self.dr
        ident = t["identb"]
        off = (lambda cc: 0 if cc >= 4 else 1)
        _, qskt = self.dma("y", t["qscol"][:], dr[f"qsk{l}"], sem=self.bsem["qscol"])
        self.wait("v", self.bsem["qscol"], qskt)
        for g in range(8):
            qt_ = self.load_w("wql", self.ar("wql").rearrange("p (c w) -> p c w", c=8),
                              dr[f"qkv{l}"][:, :, g * 128:(g + 1) * 128])
            kt_ = self.load_w("wkl", self.ar("wkl").rearrange("p (c w) -> p c w", c=8),
                              dr[f"qkv{l}"][:, :, 1024 + g * 128: 1024 + (g + 1) * 128])
            vt_ = self.load_w("wvl", self.ar("wvl").rearrange("p (c w) -> p c w", c=8),
                              dr[f"qkv{l}"][:, :, 2048 + g * 128: 2048 + (g + 1) * 128])
            self.qk_produce("wql", t["qTg"], qt_, None, off)
            self.qk_produce("wkl", t["kTg"], kt_, None, off)
            self.v_produce(vt_, off, g, bias=False)
            if g == 0 and l == 0 and os.environ.get("MK_DBG") == "4":
                self.vgap()
                self.em("v", lambda e: e.tensor_copy(t["dbgt"][:, 0:128], t["qTg"][:, 0:128]))
                self.em("v", lambda e: e.tensor_copy(t["dbgt"][:, 128:256], t["kTg"][:, 0:128]))
            dump5 = (g == 0 and l == 0 and os.environ.get("MK_DBG") == "5")
            for TgH, sci, cst, ssn, ivrow in ((t["qTg"], 0, "cs2q", "ssnq", t["invh"]),
                                              (t["kTg"], 1, "cs2k", "ssnk", t["invl"])):
                Tg = TgH[:, :]
                scol = t["qscol"][:, sci:sci + 1]
                self.em("v", lambda e, a=Tg: e.tensor_mul(t["sqb"][:], a, a))
                vt = self.tickd("v")
                self.wait("t", self.sV, vt)
                if dump5:
                    self.em("v", lambda e, a=Tg, sci=sci: e.tensor_copy(
                        t["dbgt"][:, 40 + sci * 20:60 + sci * 20], a[:, 0:20]))
                    self.em("v", lambda e, sci=sci: e.tensor_copy(
                        t["dbgt"][:, 80 + sci * 20:100 + sci * 20], t["sqb"][:, 0:20]))
                    if sci == 0:
                        self.em("v", lambda e: e.tensor_copy(t["dbgt"][:, 0:33], t["bd33b"][:, 0:33]))
                prev = 0
                for ng in range(4):
                    mt = self.tick("t", lambda e, ng=ng: e.matmul(
                        pb[6].ap()[0:33, 0:512], t["bd33b"][:], t["sqb"][:, ng * 512:(ng + 1) * 512],
                        start=True, stop=True))
                    self.wait("v", self.sT, mt)
                    self.wait("v", self.sV, prev)
                    if dump5 and ng == 0:
                        self.em("v", lambda e, sci=sci: e.tensor_copy(
                            t["dbgt"][0:33, 120 + sci * 20:140 + sci * 20], pb[6].ap()[0:33, 0:20]))
                    self.em("v", lambda e, ng=ng: e.tensor_copy(
                        t["rowS"][0:1, ng * 512:(ng + 1) * 512], pb[6].ap()[0:1, 0:512]))
                    self.em("v", lambda e, ng=ng: e.tensor_copy(
                        t["rowQ"][0:1, ng * 512:(ng + 1) * 512], pb[6].ap()[32:33, 0:512]))
                    prev = self.tickd("v")
                    self.wait("t", self.sV, prev)
                if dump5:
                    self.em("v", lambda e, sci=sci: e.tensor_copy(
                        t["dbgt"][0:1, 520 + sci * 40:520 + sci * 40 + 20], t["rowS"][0:1, 0:20]))
                    self.em("v", lambda e, sci=sci: e.tensor_copy(
                        t["dbgt"][0:1, 540 + sci * 40:540 + sci * 40 + 20], t["rowQ"][0:1, 0:20]))
                    prev = self.tickd("v")
                self.wait("s", self.sV, prev)
                self.em("s", lambda e: e.activation(t["rowS"][:], t["rowS"][:], AF.Ln))
                self.em("s", lambda e: e.activation(t["rowQ"][:], t["rowQ"][:], AF.Ln))
                self.sgap()
                self.em("s", lambda e: e.activation(t["rowS"][:], t["rowS"][:], AF.Exp, scale=-0.5))
                self.em("s", lambda e: e.activation(t["rowQ"][:], t["rowQ"][:], AF.Exp, scale=-0.5))
                st = self.tickd("s")
                self.wait("v", self.sS, st)
                self.em("v", lambda e: e.tensor_scalar_min(t["rowS"][:], t["rowS"][:], 1e12))
                self.em("v", lambda e: e.tensor_scalar_min(t["rowQ"][:], t["rowQ"][:], 1e12))
                self.vgap()
                if dump5:
                    self.em("v", lambda e, sci=sci: e.tensor_copy(
                        t["dbgt"][0:1, 160 + sci * 20:180 + sci * 20], t["rowS"][0:1, 0:20]))
                    self.em("v", lambda e, sci=sci: e.tensor_copy(
                        t["dbgt"][0:1, 200 + sci * 20:220 + sci * 20], t["rowQ"][0:1, 0:20]))
                    self.vgap()
                self.em("v", lambda e, r=ivrow: e.tensor_copy(r[0:1, :], t["rowS"][:]))
                self.em("v", lambda e, r=ivrow: e.tensor_copy(t["rsl"][0:1, :], t["rowQ"][:]))
                vt = self.tickd("v")
                self.wait("t", self.sV, vt)
                for ng in range(4):
                    c0 = ng * 512
                    self.em("t", lambda e, c0=c0, r=ivrow: e.matmul(
                        pb[5].ap()[:], t["selh0b"][:], r[0:1, c0:c0 + 512], start=True, stop=False))
                    mt = self.tick("t", lambda e, c0=c0: e.matmul(
                        pb[5].ap()[:], t["selh1b"][:], t["rsl"][0:1, c0:c0 + 512], start=False, stop=True))
                    self.wait("v", self.sT, mt)
                    if dump5 and ng == 0:
                        self.em("v", lambda e, sci=sci: e.tensor_copy(
                            t["dbgt"][:, 240 + sci * 20:260 + sci * 20], pb[5].ap()[:, 0:20]))
                    sl = Tg[:, c0:c0 + 512]
                    et = self.tick("v", lambda e, a=sl, sc_=scol: e.scalar_tensor_tensor(
                        a, a, sc_, pb[5].ap()[:], op0=ALU.mult, op1=ALU.mult))
                    if dump5 and ng == 0:
                        self.vgap()
                        self.em("v", lambda e, a=sl, sci=sci: e.tensor_copy(
                            t["dbgt"][:, 280 + sci * 20:300 + sci * 20], a[:, 0:20]))
                    et2 = self.tickd("v")
                    self.wait("t", self.sV, et2)
                CS, SN = t[cst][:, :], t[ssn][:, :]
                rot = self.ar("rot")
                for ng in range(4):
                    mt = self.tick("t", lambda e, ng=ng, a=Tg: e.matmul(
                        pb[6].ap()[:], t["protTb"][:], a[:, ng * 512:(ng + 1) * 512],
                        start=True, stop=True))
                    self.wait("v", self.sT, mt)
                    self.em("v", lambda e, ng=ng, SN=SN: e.tensor_mul(
                        rot[:, ng * 512:(ng + 1) * 512], pb[6].ap()[:], SN[:, ng * 512:(ng + 1) * 512]))
                    et = self.tickd("v")
                    self.wait("t", self.sV, et)
                dump6 = (g == 0 and l == 0 and sci == 0 and os.environ.get("MK_DBG") == "6")
                if dump6:
                    self.em("v", lambda e, CS=CS: e.tensor_copy(t["dbgt"][:, 0:128], CS[:, 0:128]))
                    self.em("v", lambda e, SN=SN: e.tensor_copy(t["dbgt"][:, 128:256], SN[:, 0:128]))
                    self.em("v", lambda e, rot=rot: e.tensor_copy(t["dbgt"][:, 256:384], rot[:, 0:128]))
                for ng in range(4):
                    self.em("v", lambda e, a=Tg, CS=CS, ng=ng: e.tensor_mul(
                        a[:, ng * 512:(ng + 1) * 512], a[:, ng * 512:(ng + 1) * 512],
                        CS[:, ng * 512:(ng + 1) * 512]))
                self.vgap()
                if dump6:
                    self.em("v", lambda e, a=Tg: e.tensor_copy(t["dbgt"][:, 384:512], a[:, 0:128]))
                    self.vgap()
                for ng in range(4):
                    self.em("v", lambda e, a=Tg, rot=rot, ng=ng: e.tensor_add(
                        a[:, ng * 512:(ng + 1) * 512], a[:, ng * 512:(ng + 1) * 512],
                        rot[:, ng * 512:(ng + 1) * 512]))
                if dump6:
                    self.vgap()
                    self.em("v", lambda e, a=Tg: e.tensor_copy(t["dbgt"][:, 512:600], a[:, 0:88]))
                if g == 0 and l == 0 and os.environ.get("MK_DBG") == "4":
                    self.vgap()
                    o0 = 256 + sci * 128
                    self.em("v", lambda e, a=Tg, o0=o0: e.tensor_copy(t["dbgt"][:, o0:o0 + 128], a[:, 0:128]))
                    self.em("v", lambda e, r=ivrow, sci=sci: e.tensor_copy(
                        t["dbgt"][0:1, 544 + sci * 16:544 + sci * 16 + 16], r[0:1, 0:16]))
                vt = self.tickd("v")
                self.wait("t", self.sV, vt)
            for tg in range(4):
                for ti in range(4):
                    tt = tg * 4 + ti
                    lo = 128 * max(0, 2 - tt)
                    kc0 = max(0, tt - 2) * 128
                    sb_ = pb[2 + (ti % 2)]
                    for hp in range(2):
                        self.em("t", lambda e, hp=hp, tt=tt, lo=lo, kc0=kc0, b=sb_: e.matmul(
                            b.ap()[:, lo:384],
                            t["qTg"][hp * 64:hp * 64 + 64, tt * 128:(tt + 1) * 128],
                            t["kTg"][hp * 64:hp * 64 + 64, kc0:tt * 128 + 128],
                            start=True, stop=False))
                        if tt >= 2:
                            self.em("t", lambda e, b=sb_: e.matmul(
                                b.ap()[:, 0:128], t["m0Tb"][:], ident[:], start=False, stop=False))
                        mt = self.tick("t", lambda e, b=sb_: e.matmul(
                            b.ap()[:, 256:384], t["m2Tb"][:], ident[:], start=False, stop=True))
                        self.wait("v", self.sT, mt)
                        vt = self.tick("v", lambda e, hp=hp, lo=lo, b=sb_: e.reduce_max(
                            t["SS"][:, hp:hp + 1], b.ap()[:, lo:384], AX.X, negate=True))
                        self.wait("s", self.sV, vt)
                        st = self.tick("s", lambda e, hp=hp, lo=lo, b=sb_: e.activation(
                            t["junkb"][:, lo:384], b.ap()[:, lo:384], AF.Exp,
                            bias=t["SS"][:, hp:hp + 1], accum_out=t["SS"][:, 2 + hp:3 + hp]))
                        self.wait("t", self.sS, st)
                    self.sgap()
                    st = self.tick("s", lambda e: e.activation(t["SS"][:, 4:6], t["SS"][:, 2:4], AF.Ln))
                    self.wait("v", self.sS, st)
                    self.em("v", lambda e: e.tensor_sub(t["SS"][:, 6:8], t["SS"][:, 0:2], t["SS"][:, 4:6]))
                    self.vgap()
                    self.em("v", lambda e: e.tensor_copy(t["SSb"][:, 0:2], t["SS"][:, 6:8]))
                    self.vgap()
                    self.em("v", lambda e: e.tensor_sub(t["SSb"][:, 2:4], t["SS"][:, 6:8], t["SSb"][:, 0:2]))
                    vt = self.tickd("v")
                    self.wait("t", self.sV, vt)
                    mt = self.tick("t", lambda e: e.transpose(
                        self.pbb.ap()[0:4, 0:128], t["SSb"][:], ident[:]))
                    self.wait("v", self.sT, mt)
                    et = self.tick("v", lambda e, tt=tt: e.tensor_copy(
                        t["shiftT"][:, tt * 128:(tt + 1) * 128], self.pbb.ap()[0:4, 0:128]))
                    self.wait("t", self.sV, et)
                    if g == 0 and l == 0 and tt == 3 and os.environ.get("MK_DBG") == "4":
                        self.vgap()
                        self.em("v", lambda e: e.tensor_copy(t["dbgt"][:, 512:520], t["SS"][:, 0:8]))
                for ti in range(4):
                    tt = tg * 4 + ti
                    j0 = max(0, 2 - tt)
                    lo = j0 * 128
                    tb = pb[2 + (ti % 2)]
                    for hp in range(2):
                        for j in range(j0, 3):
                            w = tt - 2 + j
                            msk = t["m0b"] if (j == j0 and tt >= 2) else (t["m2b"] if j == 2 else None)
                            self.em("t", lambda e, hp=hp, tt=tt, w=w, j=j, b=tb: e.matmul(
                                b.ap()[:, j * 128:(j + 1) * 128],
                                t["kTg"][hp * 64:hp * 64 + 64, w * 128:(w + 1) * 128],
                                t["qTg"][hp * 64:hp * 64 + 64, tt * 128:(tt + 1) * 128],
                                start=True, stop=False))
                            self.em("t", lambda e, hp=hp, tt=tt, j=j, b=tb, last=(msk is None): e.matmul(
                                b.ap()[:, j * 128:(j + 1) * 128],
                                t["sel4b"][:, hp * 128:(hp + 1) * 128],
                                t["shiftT"][:, tt * 128:(tt + 1) * 128],
                                start=False, stop=last))
                            if msk is not None:
                                self.em("t", lambda e, m=msk, j=j, b=tb: e.matmul(
                                    b.ap()[:, j * 128:(j + 1) * 128], m[:], ident[:],
                                    start=False, stop=True))
                        mt = self.tickd("t")
                        self.wait("s", self.sT, mt)
                        st = self.tick("s", lambda e, lo=lo, b=tb: e.activation(
                            t["pT"][:, lo:384], b.ap()[:, lo:384], AF.Exp))
                        self.wait("t", self.sS, st)
                        if g == 0 and l == 0 and tt == 3 and hp == 0 and os.environ.get("MK_DBG") == "4":
                            self.wait("v", self.sS, st)
                            self.vgap()
                            self.em("v", lambda e: e.tensor_copy(t["dbgt"][0:64, 576:600], t["pT"][0:64, 256:280]))
                            dt_ = self.tickd("v")
                            self.wait("s", self.sV, dt_)
                        ab = pb[4 + hp]
                        for j in range(j0, 3):
                            w = tt - 2 + j
                            vsl = self.ar("vtokg", (w * 128 + hp * 64, w * 128 + hp * 64 + 64))
                            self.em("t", lambda e, v_=vsl, j=j, ti=ti, ab=ab, j0=j0: e.matmul(
                                ab.ap()[0:64, ti * 128:(ti + 1) * 128],
                                v_, t["pT"][:, j * 128:(j + 1) * 128],
                                start=(j == j0), stop=(j == 2)))
                        mt = self.tickd("t")
                mt = self.tickd("t")
                self.wait("v", self.sT, mt)
                self.wait("v", self.sIO, self._stage_io)
                self.em("v", lambda e: e.tensor_copy(t["stage"][0:64, 0:512], pb[4].ap()[0:64, :]))
                et = self.tick("v", lambda e: e.tensor_copy(t["stage"][0:64, 512:1024], pb[5].ap()[0:64, :]))
                self.wait("t", self.sV, et)
                self.wait("y", self.sV, et)
                self.dma("y", dr["aob"][g * 128:g * 128 + 64, tg * 512:(tg + 1) * 512],
                         t["stage"][0:64, 0:512])
                _, siot = self.dma("y", dr["aob"][g * 128 + 64:g * 128 + 128, tg * 512:(tg + 1) * 512],
                                   t["stage"][0:64, 512:1024])
                self._stage_io = siot
        self.bar()
        self.proj_residual(dr[f"wo{l}"], None, 0, refresh_xb=True)
        self.bar()

    def ph_inner_ffn(self, l):
        t, pb, dr = self.t, self.pb, self.dr
        self.dma("y", t["b1a"][:], dr[f"b1a{l}"], sem=self.bsem["lparam"])
        self.dma("y", t["b1g"][:], dr[f"b1g{l}"], sem=self.bsem["lparam"])
        lpt = 0
        for i, nm in enumerate((f"b2_{l}", f"lng{l}", f"lnb{l}")):
            _, lpt = self.dma("y", t["lparam"][:, i * 8:(i + 1) * 8], dr[nm], sem=self.bsem["lparam"])
        self.wait("v", self.bsem["lparam"], lpt)
        self.wait("s", self.bsem["lparam"], lpt)
        rt = self.ln_rows(shifted=True)
        self.wait("t", self.sV, rt)
        for nb in (3, 2, 1, 0):
            self.ln_apply_xn(nb)
            hg_io_prev = 0
            for hct in range(HCT):
                wan = "wa0" if hct % 2 == 0 else "wa1"
                wgn = "wg0" if hct % 2 == 0 else "wg1"
                at_ = self.load_w(wan, self.ar(wan).rearrange("p (c x) -> p c x", c=8),
                                  dr[f"w1a{l}"][:, :, hct * 128:(hct + 1) * 128])
                gt_ = self.load_w(wgn, self.ar(wgn).rearrange("p (c x) -> p c x", c=8),
                                  dr[f"w1g{l}"][:, :, hct * 128:(hct + 1) * 128])
                self.wait("t", at_[0], at_[1])
                self.wait("t", gt_[0], gt_[1])
                ab, gb = pb[2 + (hct % 2) * 2], pb[3 + (hct % 2) * 2]
                for cc in range(CT):
                    self.em("t", lambda e, wn=wan, cc=cc, st=(cc == 0), b=ab:
                            e.matmul(b.ap()[:], self.ar(wn)[:, cc * 128: cc * 128 + 128],
                                     self.ar("xn", (cc * 512, cc * 512 + 512)),
                                     start=st, stop=(cc == CT - 1)))
                for cc in range(CT):
                    self.em("t", lambda e, wn=wgn, cc=cc, st=(cc == 0), b=gb:
                            e.matmul(b.ap()[:], self.ar(wn)[:, cc * 128: cc * 128 + 128],
                                     self.ar("xn", (cc * 512, cc * 512 + 512)),
                                     start=st, stop=(cc == CT - 1)))
                mt = self.tickd("t")
                self.mark_w(wan, mt)
                self.mark_w(wgn, mt)
                self.wait("s", self.sT, mt)
                st = self.tick("s", lambda e, hct=hct, b=gb: e.activation(
                    t["gelu"][:], b.ap()[:], AF.Gelu, bias=t["b1g"][:, hct:hct + 1]))
                self.wait("v", self.sS, st)
                self.wait("v", self.sIO, hg_io_prev)
                hgs = self.ar("hgs0" if hct % 2 == 0 else "hgs1")
                et = self.tick("v", lambda e, hct=hct, b=ab, hgs=hgs: e.scalar_tensor_tensor(
                    hgs, b.ap()[:], t["b1a"][:, hct:hct + 1], t["gelu"][:],
                    op0=ALU.add, op1=ALU.mult))
                self.wait("t", self.sV, et)
                self.wait("y", self.sV, et)
                _, hg_io_prev = self.dma("y", dr["hgb"][hct * 128:(hct + 1) * 128, :], hgs)
            self.wait("y", self.sIO, hg_io_prev)
            for pass_, ocs in (("A", list(range(7))), ("B", [7])):
                for hct in range(HCT):
                    wbn = "w20" if hct % 2 == 0 else "w21"
                    hrn = "hgr0" if hct % 2 == 0 else "hgr1"
                    cols = (0, 1024) if pass_ == "A" else (896, 1024)
                    wt = self.load_w(wbn, self.ar(wbn, (0, cols[1] - cols[0])),
                                     dr[f"w2{l}"][:, hct, cols[0]:cols[1]])
                    self.wait("y", self.sT, self.wlast.get(hrn, 0))
                    _, ht = self.dma("y", self.ar(hrn), dr["hgb"][hct * 128:(hct + 1) * 128, :],
                                     sem=self.bsem[hrn])
                    self.wait("t", wt[0], wt[1])
                    self.wait("t", self.bsem[hrn], ht)
                    for oc in ocs:
                        bi = oc if pass_ == "A" else 0
                        wcol = oc * 128 - cols[0] if pass_ == "A" else 0
                        self.em("t", lambda e, wn=wbn, hn=hrn, wcol=wcol, st=(hct == 0), bi=bi:
                                e.matmul(pb[bi].ap()[:], self.ar(wn, (wcol, wcol + 128)),
                                         self.ar(hn), start=st, stop=False))
                    mwt = self.tickd("t")
                    self.mark_w(wbn, mwt)
                    self.mark_w(hrn, mwt)
                mt = self.tickd("t")
                self.wait("v", self.sT, mt)
                for oc in ocs:
                    bi = oc if pass_ == "A" else 0
                    xfa = t[f"XF{oc}"][:, nb * 512 + 1: nb * 512 + 513]
                    bcol = t["lparam"][:, oc:oc + 1]
                    self.em("v", lambda e, o=xfa, b=pb[bi], bc=bcol: e.scalar_tensor_tensor(
                        o, b.ap()[:], bc, o, op0=ALU.add, op1=ALU.add))
                if len(ocs) == 1:
                    self.vgap()
                for oc in ocs:
                    self.em("v", lambda e, oc=oc, nb=nb: e.tensor_copy(
                        t[f"XB{oc}"][:, nb * 512 + 1: nb * 512 + 513],
                        t[f"XF{oc}"][:, nb * 512 + 1: nb * 512 + 513]))
                vt = self.tickd("v")
                self.wait("t", self.sV, vt)
        self.bar()

    def ph_final(self, full):
        t, dr = self.t, self.dr
        self.bar()
        if full:
            for cc in range(CT):
                for ng in range(4):
                    _, lt = self.dma("y", t["scr"][:], dr["x1sav"][cc * 128:(cc + 1) * 128, ng * 512:(ng + 1) * 512])
                    self.wait("v", self.sIO, lt)
                    xfa = t[f"XF{cc}"][:, ng * 512 + 1: ng * 512 + 513]
                    self.em("v", lambda e, a=xfa: e.tensor_add(a, a, t["scr"][:]))
                    self.vgap()
                    self.em("v", lambda e, cc=cc, ng=ng: e.tensor_copy(
                        t[f"XB{cc}"][:, ng * 512 + 1: ng * 512 + 513],
                        t[f"XF{cc}"][:, ng * 512 + 1: ng * 512 + 513]))
                    vt = self.tickd("v")
                    self.wait("y", self.sV, vt)
            self.bar()
            self.ln_apply_block(save_x1=False)
            self.bar()
        for cc in range(CT):
            self.dma("y", dr["yT"][cc * 128:(cc + 1) * 128, :], t[f"XF{cc}"][:, 1:N + 1])
        self.dma("y", dr["dbg"][:, 0:600], t["dbgt"][:])
        self.wait("y", self.sIO, self.sIO.n)
        self.bar()

    def finish(self):
        emE = self.E
        with self.ctx:
            with self.nc.Block() as blk:
                @blk.tensor
                def _(eng):
                    for f in emE["t"]:
                        f(eng)

                @blk.vector
                def _(eng):
                    for f in emE["v"]:
                        f(eng)

                @blk.scalar
                def _(eng):
                    for f in emE["s"]:
                        f(eng)

                @blk.gpsimd
                def _(eng):
                    for f in emE["p"]:
                        f(eng)

                @blk.sync
                def _(eng):
                    for f in emE["y"]:
                        f(eng)
        return self.nc


def build(n_sub):
    b = B()
    with contextlib.ExitStack() as gs:
        b.mk_gen_scratch(gs)
        b.ph_load_pre()
        if MK_PHASE >= 1:
            b.ph_tables()
    b.mk_core2()
    b.ph_load_post()
    if MK_PHASE < 2:
        b.ph_final(full=False)
        return b.finish()
    b.ph_performer()
    if MK_PHASE < 3:
        b.ph_final(full=False)
        return b.finish()
    b.ln_apply_block(save_x1=True)
    b.bar()
    if MK_PHASE < 4:
        b.ph_final(full=False)
        return b.finish()
    done = 0
    for l in range(6):
        if done >= n_sub:
            break
        b.ph_inner_attn(l)
        done += 1
        if done >= n_sub:
            break
        b.ph_inner_ffn(l)
        done += 1
    b.ph_final(full=(n_sub >= 12))
    return b.finish()


DBG_OUT = {}


def kernel(**inputs):
    P = host_arrays(inputs)
    x = np.asarray(inputs["L0_attn_x"])
    nc = build(N_SUB)
    in_maps = []
    for c in range(8):
        m = {"xT": np.ascontiguousarray(x[c].T)}
        m.update(P)
        in_maps.append(m)
    res = run_bass_kernel_spmd(nc, in_maps, core_ids=list(range(8)))
    if os.environ.get("MK_DBG"):
        DBG_OUT["dbg"] = np.asarray(res.results[0]["dbg"])
    out = np.stack([np.ascontiguousarray(np.asarray(r["yT"]).T) for r in res.results], axis=0)
    return out.astype(np.float32)
